# version 01 (final)
# baseline (speedup 1.0000x reference)
import numpy as np
import ml_dtypes
import concourse.bass as bass
import concourse.mybir as mybir
from concourse.bass_utils import run_bass_kernel_spmd

F32 = mybir.dt.float32
BF16 = mybir.dt.bfloat16
AF = mybir.ActivationFunctionType
ALU = mybir.AluOpType
AX = mybir.AxisListType

D = 2048
P = 2048
TB = 16
HC = 16
E = 8
OC = 16
NRO = 16384
S_FULL = 4096
SCALE = 1.0 / float(np.sqrt(2048.0))
EPS = 1e-5

ENGINES = ["sync", "scalar", "vector", "tensor", "gpsimd"]
SEMS = ["sS", "sA", "sP", "sC", "sV", "sG", "sX"]


class Ctx:

    def __init__(self, nc, en, eng, sems):
        self.nc, self.en, self.e = nc, en, eng
        self.sems = sems
        self.cnt = {k: 0 for k in sems}
        self.waited = {k: 0 for k in sems}

    def do(self, en, emit, waits=(), inc=None, n=1):
        if en == self.en:
            for s, v in waits:
                if v > self.waited[s]:
                    self.e.wait_ge(self.sems[s], v)
                    self.waited[s] = v
            ins = emit(self.e)
            if inc:
                ins.then_inc(self.sems[inc], n)
        if inc:
            self.cnt[inc] += n
            return self.cnt[inc]
        return None

    def dma(self, en, out, in_, waits=()):
        sem = "sS" if en == "sync" else "sA"
        return self.do(en, lambda e: e.dma_start(out=out, in_=in_), waits, sem, 16)

    def barrier(self):
        snap = dict(self.cnt)
        if self.en is not None:
            for s, v in snap.items():
                if v > self.waited[s]:
                    self.e.wait_ge(self.sems[s], v)
                    self.waited[s] = v


class Carver:
    def __init__(self, arena, n, off=0):
        self.a, self.n, self.off = arena, n, off

    def f32(self, *shape, parts=128):
        ne = int(np.prod(shape))
        assert self.off + ne <= self.n, f"arena overflow {self.off}+{ne}>{self.n}"
        ap = self.a[0:parts, self.off:self.off + ne]
        self.off += ne
        if len(shape) > 1:
            names = " ".join(f"d{i}" for i in range(len(shape)))
            ap = ap.rearrange(f"p ({names}) -> p {names}",
                              **{f"d{i}": s for i, s in enumerate(shape)})
        return ap

    def bf16(self, *shape, parts=128):
        ne = int(np.prod(shape))
        assert ne % 2 == 0 and self.off + ne // 2 <= self.n
        ap = self.a[0:parts, self.off:self.off + ne // 2].bitcast(BF16)
        self.off += ne // 2
        if len(shape) > 1:
            names = " ".join(f"d{i}" for i in range(len(shape)))
            ap = ap.rearrange(f"p ({names}) -> p {names}",
                              **{f"d{i}": s for i, s in enumerate(shape)})
        return ap


CBLK_OFF = 16384
PH_OFF = 16384 + 3328


def build(phases=("ffn1", "attn", "ffn2"), dump=None):
    nc = bass.Bass("TRN2", target_bir_lowering=False)

    x_d = nc.dram_tensor("x", [P, D], F32, kind="ExternalInput")
    rwt_d = nc.dram_tensor("rwt", [D, NRO], F32, kind="ExternalInput")
    rbias_d = nc.dram_tensor("rbias", [128, 128], F32, kind="ExternalInput")
    ewt_d = nc.dram_tensor("ewt", [E, D, D], F32, kind="ExternalInput")
    eb_d = nc.dram_tensor("eb", [E, D], F32, kind="ExternalInput")
    qkv_d = nc.dram_tensor("qkvw", [D, 3 * D], F32, kind="ExternalInput")
    wo_d = nc.dram_tensor("wo", [D, D], F32, kind="ExternalInput")
    gam_d = nc.dram_tensor("gam", [128, 16], F32, kind="ExternalInput")
    bet_d = nc.dram_tensor("bet", [128, 16], F32, kind="ExternalInput")
    mask_d = nc.dram_tensor("maskrows", [TB, 128, S_FULL], BF16, kind="ExternalInput")
    out_d = nc.dram_tensor("out", [P, D], F32, kind="ExternalOutput")

    r_dram = nc.dram_tensor("r_scratch", [P, D], F32, kind="Internal")
    q_dram = nc.dram_tensor("q_scratch", [4, HC, 128, 512], BF16, kind="Internal")
    ao_dram = nc.dram_tensor("ao_scratch", [D, P], BF16, kind="Internal")
    kv_own = nc.dram_tensor("kv_own", [2 * D, D], BF16, kind="Internal")
    kv_all = nc.dram_tensor("kv_all", [4 * D, D], BF16, kind="Internal")

    ident_np = np.eye(128, dtype=np.float32)
    ident_bf_d = nc.inline_tensor(ident_np.astype(ml_dtypes.bfloat16), "identbf")
    ident_f32_d = nc.inline_tensor(ident_np, "identf32")

    ident = nc.alloc_sbuf_tensor("ident", [128, 128], BF16)
    identf = nc.alloc_sbuf_tensor("identf", [128, 128], F32)
    ecol = nc.alloc_sbuf_tensor("ecol", [128, E, E], BF16)
    rbias = nc.alloc_sbuf_tensor("rbias_sb", [128, 128], F32)
    gam = nc.alloc_sbuf_tensor("gam_sb", [128, 16], F32)
    bet = nc.alloc_sbuf_tensor("bet_sb", [128, 16], F32)
    stats = nc.alloc_sbuf_tensor("stats", [128, 48], F32)
    eb_bf = nc.alloc_sbuf_tensor("eb_bf", [E, D], BF16)
    ARENA_N = 49152
    arena = nc.alloc_sbuf_tensor("arena", [128, ARENA_N], F32)
    assert nc.sbuf_bytes_remaining >= 0, f"SBUF over by {-nc.sbuf_bytes_remaining}B"

    ST_MEAN, ST_VAR, ST_RSTD, ST_LNB, ST_EPS, ST_NEGMAX, ST_RSUM = 0, 1, 2, 3, 4, 5, 6
    ST_RDEN = 8
    ST_BN = 16

    ps = nc.alloc_psum_tensor("psf", [128, 6, 512], F32)
    ptb = nc.alloc_psum_tensor("pst", [128, 2, 1024], BF16)

    sems = {s: nc.alloc_semaphore(s) for s in SEMS}

    def emit_all(ctx: Ctx):
        tr = {"i": 0, "done": {}, "tdone": {}}

        def ptranspose(src_ap, dst_fn, w_tr=(), w_ev=(), idmat=None, ow=128):
            i = tr["i"]; tr["i"] += 1
            slot = i % 2
            w = list(w_tr)
            if i >= 2:
                w.append(("sC", tr["done"][i - 2]))
            vt = ctx.do("tensor", lambda e: e.transpose(
                ptb[:, slot, 0:ow], src_ap, idmat if idmat is not None else ident[:, :]),
                w, "sP")
            tr["tdone"][i] = vt
            ve = ctx.do("scalar", lambda e: dst_fn(e, ptb[:, slot, 0:ow]),
                        [("sP", vt)] + list(w_ev), "sC")
            tr["done"][i] = ve
            return vt, ve

        ctx.dma("sync", ident[:, :], ident_bf_d.ap())
        ctx.dma("sync", identf[:, :], ident_f32_d.ap())
        ctx.dma("sync", rbias[:, :], rbias_d.ap())
        ctx.dma("sync", gam[:, :], gam_d.ap())
        ctx.dma("sync", bet[:, :], bet_d.ap())
        cpro = Carver(arena, ARENA_N)
        ebst = cpro.f32(D, parts=8)
        v_pro = ctx.dma("sync", ebst, eb_d.ap())
        ctx.do("vector", lambda e: e.memset(ecol[:, :, :], 0.0), inc="sV")
        for ee in range(E):
            ctx.do("vector", lambda e, ee=ee: e.memset(ecol[:, ee, ee:ee + 1], 1.0), inc="sV")
        ctx.do("vector", lambda e: e.tensor_copy(eb_bf[:, :], ebst),
               waits=[("sS", v_pro)], inc="sV")
        ctx.barrier()

        def ln_phase(src_dram, with_gb):
            cv = Carver(arena, ARENA_N, PH_OFF)
            h_fm = Carver(arena, ARENA_N).bf16(HC, 2048)
            rblk = [cv.f32(2048), cv.f32(2048)]
            xnb = [cv.bf16(2048), cv.bf16(2048)]
            xsc = cv.f32(2048)
            v_dma, v_xn, v_trl, v_sqd = {}, {}, {}, {}
            for tb in range(TB + 1):
                if tb < TB:
                    w = []
                    if tb >= 2:
                        w.append(("sC", v_xn[tb - 2]))
                    v_dma[tb] = ctx.dma("scalar", rblk[tb % 2],
                                        src_dram.ap()[tb * 128:(tb + 1) * 128, :], w)
                if tb == 0:
                    continue
                t = tb - 1
                w = [("sA", v_dma[t])]
                if t >= 1:
                    w.append(("sC", v_sqd[t - 1]))
                v_rs = ctx.do("vector", lambda e, t=t: e.reduce_sum(
                    stats[:, ST_BN:ST_BN + 1], rblk[t % 2], AX.X), w, "sV")
                w = [("sA", v_dma[t])]
                if t >= 2:
                    w.append(("sP", v_trl[t - 2]))
                v_sqr = ctx.do("scalar", lambda e, t=t: e.activation(
                    xnb[t % 2], rblk[t % 2], AF.Square), w, "sC")
                v_sqd[t] = v_sqr
                v_rs2 = ctx.do("vector", lambda e, t=t: e.reduce_sum(
                    stats[:, ST_BN + 1:ST_BN + 2], xnb[t % 2], AX.X),
                    [("sC", v_sqr)], "sV")
                def f_mv(e):
                    e.tensor_scalar(stats[:, ST_MEAN:ST_MEAN + 1],
                                    stats[:, ST_BN:ST_BN + 1], 1.0 / D, None, ALU.mult)
                    e.tensor_scalar(stats[:, ST_VAR:ST_VAR + 1],
                                    stats[:, ST_BN + 1:ST_BN + 2], 1.0 / D, None, ALU.mult)
                    e.tensor_tensor(stats[:, ST_LNB:ST_LNB + 1],
                                    stats[:, ST_MEAN:ST_MEAN + 1],
                                    stats[:, ST_MEAN:ST_MEAN + 1], ALU.mult)
                    e.tensor_tensor(stats[:, ST_VAR:ST_VAR + 1],
                                    stats[:, ST_VAR:ST_VAR + 1],
                                    stats[:, ST_LNB:ST_LNB + 1], ALU.subtract)
                    return e.tensor_scalar(stats[:, ST_VAR:ST_VAR + 1],
                                           stats[:, ST_VAR:ST_VAR + 1], EPS, None,
                                           ALU.add)
                v_mv = ctx.do("vector", f_mv, [("sV", v_rs2)], "sV")
                v_sq = ctx.do("scalar", lambda e: e.activation(
                    stats[:, ST_RSTD:ST_RSTD + 1], stats[:, ST_VAR:ST_VAR + 1],
                    AF.Sqrt), [("sV", v_mv)], "sC")
                def f_fin(e):
                    e.reciprocal(stats[:, ST_RSTD:ST_RSTD + 1], stats[:, ST_RSTD:ST_RSTD + 1])
                    return e.tensor_scalar(stats[:, ST_LNB:ST_LNB + 1],
                                           stats[:, ST_MEAN:ST_MEAN + 1], -1.0, None,
                                           ALU.mult)
                v_fin = ctx.do("vector", f_fin, [("sC", v_sq)], "sV")
                v_x1p = ctx.do("scalar", lambda e, t=t: e.activation(
                    xsc, rblk[t % 2], AF.Identity,
                    bias=stats[:, ST_LNB:ST_LNB + 1]), [("sV", v_fin)], "sC")
                v_xn[t] = ctx.do("scalar", lambda e, t=t: e.activation(
                    xnb[t % 2], xsc, AF.Identity,
                    scale=stats[:, ST_RSTD:ST_RSTD + 1]), [], "sC")
                for hc in range(HC):
                    def dst(e, psrc, t=t, hc=hc):
                        if with_gb:
                            return e.activation(h_fm[:, hc, t * 128:(t + 1) * 128], psrc,
                                                AF.Identity, scale=gam[:, hc:hc + 1],
                                                bias=bet[:, hc:hc + 1])
                        return e.activation(h_fm[:, hc, t * 128:(t + 1) * 128], psrc, AF.Copy)
                    vt, _ = ptranspose(xnb[t % 2][:, hc * 128:(hc + 1) * 128], dst,
                                       w_tr=[("sC", v_xn[t])])
                    v_trl[t] = vt
            return h_fm

        def router_phase(h_fm, cblk):
            c_sb, c_bf, c05 = cblk
            cv = Carver(arena, ARENA_N, PH_OFF)
            l_buf = cv.bf16(E, 2048)
            S = [cv.bf16(2048) for _ in range(8)]
            wbf = cv.bf16(HC, 1024)
            wst = cv.f32(HC, 256)

            ctx.do("vector", lambda e: e.memset(c_sb, 0.0), inc="sV")

            v_wdma, v_cast, v_mm, v_evac = {}, {}, {}, {}
            v_y, v_sig, v_con, v_ones, v_cacc = {}, {}, {}, {}, {}

            def wdma(oc, ep):
                i = oc * 4 + ep
                w = []
                if i >= 1:
                    w.append(("sG", v_cast[i - 1]))
                col0 = oc * 1024 + ep * 256
                v_wdma[i] = ctx.dma("sync", wst,
                                    rwt_d.ap()[:, col0:col0 + 256].rearrange(
                                        "(hc p) c -> p hc c", p=128), w)

            def wcast(oc, ep):
                i = oc * 4 + ep
                w = [("sS", v_wdma[i])]
                if oc > 0:
                    w.append(("sP", v_mm[(oc - 1, 2 * ep + 1, 3)]))
                v_cast[i] = ctx.do("gpsimd", lambda e, ep=ep: e.tensor_copy(
                    wbf[:, :, ep * 256:(ep + 1) * 256], wst), w, "sG")

            wdma(0, 0)
            for oc in range(OC):
                for ep in range(4):
                    wcast(oc, ep)
                    ni = oc * 4 + ep + 1
                    if ni < OC * 4:
                        wdma(ni // 4, ni % 4)
                for e_ in range(E):
                    for tc in range(4):
                        gi = (oc * E + e_) * 4 + tc
                        bank = gi % 2
                        w = [("sG", v_cast[oc * 4 + e_ // 2])]
                        if gi >= 2:
                            w.append(("sC", v_evac[gi - 2]))
                        def f_mm(e, e_=e_, tc=tc, bank=bank):
                            for hc in range(HC):
                                ins = e.matmul(ps[:, bank, :],
                                               wbf[:, hc, e_ * 128:(e_ + 1) * 128],
                                               h_fm[:, hc, tc * 512:(tc + 1) * 512],
                                               start=(hc == 0), stop=(hc == HC - 1))
                            return ins
                        v_mm[(oc, e_, tc)] = ctx.do("tensor", f_mm, w, "sP")
                        w = [("sP", v_mm[(oc, e_, tc)])]
                        if oc > 0:
                            w.append(("sP", v_ones[(oc - 1, e_)]))
                        g = oc * 8 + e_
                        v_evac[gi] = ctx.do("scalar", lambda e, e_=e_, tc=tc, bank=bank, g=g:
                                            e.activation(l_buf[:, e_, tc * 512:(tc + 1) * 512],
                                                         ps[:, bank, :], AF.Identity,
                                                         bias=rbias[:, g:g + 1]),
                                            w, "sC")
                L = lambda e_: l_buf[:, e_, :]
                def f_tour(e):
                    tt = e.tensor_tensor
                    tt(S[0], L(0), L(1), ALU.max); tt(S[1], L(0), L(1), ALU.min)
                    tt(S[2], L(2), L(3), ALU.max); tt(S[3], L(2), L(3), ALU.min)
                    tt(S[4], L(4), L(5), ALU.max); tt(S[5], L(4), L(5), ALU.min)
                    tt(S[6], L(6), L(7), ALU.max); tt(S[7], L(6), L(7), ALU.min)
                    tt(S[1], S[1], S[3], ALU.max); tt(S[5], S[5], S[7], ALU.max)
                    tt(S[3], S[0], S[2], ALU.max); tt(S[0], S[0], S[2], ALU.min)
                    tt(S[7], S[4], S[6], ALU.max); tt(S[4], S[4], S[6], ALU.min)
                    tt(S[1], S[0], S[1], ALU.max); tt(S[5], S[4], S[5], ALU.max)
                    tt(S[2], S[3], S[7], ALU.max)
                    tt(S[0], S[3], S[7], ALU.min)
                    tt(S[1], S[1], S[5], ALU.max)
                    tt(S[0], S[0], S[1], ALU.max)
                    return tt(S[6], S[2], S[0], ALU.add)
                v_tour = ctx.do("vector", f_tour,
                                [("sC", v_evac[(oc * E + E - 1) * 4 + 3])], "sV")
                Y = [S[1], S[2]]
                G = [S[3], S[4]]
                for e_ in range(E):
                    w = [("sV", v_tour)] if e_ == 0 else []
                    if e_ >= 2:
                        w.append(("sC", v_sig[(oc, e_ - 2)]))
                    def f_y(e, e_=e_):
                        e.scalar_tensor_tensor(Y[e_ % 2], L(e_), 2.0, S[6],
                                               ALU.mult, ALU.subtract)
                        return e.tensor_tensor(L(e_), L(e_), S[0], ALU.is_ge)
                    v_y[(oc, e_)] = ctx.do("vector", f_y, w, "sV")
                    w = [("sV", v_y[(oc, e_)])]
                    if e_ >= 2:
                        w.append(("sV", v_con[(oc, e_ - 2)]))
                    v_sig[(oc, e_)] = ctx.do("scalar", lambda e, e_=e_: e.activation(
                        G[e_ % 2], Y[e_ % 2], AF.Sigmoid), w, "sC")
                    v_con[(oc, e_)] = ctx.do("vector", lambda e, e_=e_: e.tensor_tensor(
                        L(e_), L(e_), G[e_ % 2], ALU.mult),
                        [("sC", v_sig[(oc, e_)])], "sV")
                for e_ in range(E):
                    w = [("sV", v_con[(oc, e_)])]
                    if e_ == 0 and oc > 0:
                        w.append(("sV", v_cacc[oc - 1]))
                    def f_ones(e, e_=e_):
                        for tc in range(4):
                            ins = e.matmul(ps[0:8, 2 + tc, :], ecol[:, e_, :],
                                           l_buf[:, e_, tc * 512:(tc + 1) * 512],
                                           start=(e_ == 0), stop=(e_ == E - 1),
                                           skip_group_check=True)
                        return ins
                    v_ones[(oc, e_)] = ctx.do("tensor", f_ones, w, "sP")
                def f_cacc(e):
                    for tc in range(4):
                        ins = e.tensor_tensor(c_sb[:, tc * 512:(tc + 1) * 512],
                                              c_sb[:, tc * 512:(tc + 1) * 512],
                                              ps[0:8, 2 + tc, :], ALU.add)
                    return ins
                w = [("sP", v_ones[(oc, E - 1)])]
                if oc == 0:
                    w.append(("sG", ctx.cnt["sG"]))
                v_cacc[oc] = ctx.do("vector", f_cacc, w, "sV")

            v_cbf = ctx.do("vector", lambda e: e.tensor_copy(c_bf, c_sb),
                           [("sV", v_cacc[OC - 1])], "sV")
            for tb in range(TB):
                def dstc(e, psrc, tb=tb):
                    return e.activation(c05[:, tb, :], psrc, AF.Copy, scale=0.5)
                ptranspose(c_bf[:, tb * 128:(tb + 1) * 128], dstc,
                           w_tr=[("sV", v_cbf)], idmat=ident[0:8, 0:8], ow=8)
            return v_cbf

        def expert_phase(h_fm, cblk, src_dram, dst_dram):
            c_sb, c_bf, c05 = cblk
            cv = Carver(arena, ARENA_N, PH_OFF)
            facc = cv.f32(TB, 512)
            webf = [cv.bf16(HC, 512), cv.bf16(HC, 512)]
            wst = [cv.f32(HC, 256), cv.f32(HC, 256)]
            mixt = [cv.bf16(512) for _ in range(3)]

            v_fin, v_wdma, v_cast, v_mm, v_ev, v_addg, v_stq = {}, {}, {}, {}, {}, {}, {}
            NE = E + 1

            def wdma(pos, half):
                i = pos * 2 + half
                w = []
                if i >= 2:
                    w.append(("sG", v_cast[i - 2]))
                oq, e_ = divmod(pos, E)
                c0 = oq * 512 + half * 256
                v_wdma[i] = ctx.dma("sync", wst[i % 2],
                                    ewt_d.ap()[e_, :, c0:c0 + 256].rearrange(
                                        "(hc p) c -> p hc c", p=128), w)

            def wcast(pos, half):
                i = pos * 2 + half
                w = [("sS", v_wdma[i])]
                if pos >= 2:
                    oqp, ep = divmod(pos - 2, E)
                    w.append(("sP", v_mm[(oqp, ep, TB - 1)]))
                v_cast[i] = ctx.do("gpsimd", lambda e, pos=pos, half=half, i=i: e.tensor_copy(
                    webf[pos % 2][:, :, half * 256:half * 256 + 256], wst[i % 2]), w, "sG")

            wdma(0, 0); wdma(0, 1)
            for oq in range(4):
                w = [("sV", v_addg[(oq - 1) * NE * TB + NE * TB - 1])] if oq > 0 else []
                v_fin[oq] = ctx.dma("scalar", facc,
                                    src_dram.ap()[:, oq * 512:(oq + 1) * 512].rearrange(
                                        "(tb p) o -> p tb o", p=128), w)
                for e_ in range(NE):
                    k = oq * NE + e_
                    pos = oq * E + e_
                    if e_ < E:
                        for h in range(2):
                            wcast(pos, h)
                            if pos + 1 < 4 * E:
                                wdma(pos + 1, h)
                    for tb in range(TB):
                        gi = k * TB + tb
                        bank = gi % 3
                        if e_ < E:
                            w = [("sG", v_cast[pos * 2 + 1])]
                            if gi >= 3:
                                w.append(("sC", v_ev[gi - 3]))
                            def f_mm(e, e_=e_, tb=tb, bank=bank, pos=pos):
                                wb = webf[pos % 2]
                                for hc in range(HC):
                                    ins = e.matmul(ps[:, bank, :],
                                                   h_fm[:, hc, tb * 128:(tb + 1) * 128],
                                                   wb[:, hc, :],
                                                   start=(hc == 0), stop=(hc == HC - 1))
                                return ins
                        else:
                            w = []
                            if gi >= 3:
                                w.append(("sC", v_ev[gi - 3]))
                            def f_mm(e, tb=tb, bank=bank, oq=oq):
                                return e.matmul(ps[:, bank, :],
                                                c_bf[:, tb * 128:(tb + 1) * 128],
                                                eb_bf[:, oq * 512:(oq + 1) * 512],
                                                start=True, stop=True)
                        v_mm[(oq, e_, tb)] = ctx.do("tensor", f_mm, w, "sP")
                        w = [("sP", v_mm[(oq, e_, tb)])]
                        if gi >= 3:
                            w.append(("sV", v_addg[gi - 3]))
                        mslot = gi % 3
                        if e_ < E:
                            f_evv = lambda e, tb=tb, bank=bank, mslot=mslot, e_=e_: e.activation(
                                mixt[mslot], ps[:, bank, :], AF.Identity,
                                scale=c05[:, tb, e_:e_ + 1])
                        else:
                            f_evv = lambda e, bank=bank, mslot=mslot: e.activation(
                                mixt[mslot], ps[:, bank, :], AF.Copy, scale=0.5)
                        v_ev[gi] = ctx.do("scalar", f_evv, w, "sC")
                        v_addg[gi] = ctx.do("vector", lambda e, tb=tb, mslot=mslot:
                                            e.tensor_tensor(facc[:, tb, :], facc[:, tb, :],
                                                            mixt[mslot], ALU.add),
                                            [("sC", v_ev[gi]), ("sA", v_fin[oq])], "sV")
                v_stq[oq] = ctx.dma("scalar",
                                    dst_dram.ap()[:, oq * 512:(oq + 1) * 512].rearrange(
                                        "(tb p) o -> p tb o", p=128), facc,
                                    [("sV", v_addg[(oq * NE + NE - 1) * TB + TB - 1])])

        def ffn(src_dram, dst_dram):
            h_fm = ln_phase(src_dram, with_gb=False)
            ctx.barrier()
            cc = Carver(arena, ARENA_N, CBLK_OFF)
            c_sb = cc.f32(2048, parts=8)
            c_bf = cc.bf16(2048, parts=8)
            c05 = cc.f32(TB, 8)
            cblk = (c_sb, c_bf, c05)
            router_phase(h_fm, cblk)
            ctx.barrier()
            expert_phase(h_fm, cblk, src_dram, dst_dram)
            ctx.barrier()

        def qkv_phase(h_fm):
            cv = Carver(arena, ARENA_N, PH_OFF)
            wjbf = [cv.bf16(HC, 128), cv.bf16(HC, 128)]
            wvbf = [cv.bf16(HC, 512), cv.bf16(HC, 512)]
            wst = [cv.f32(HC, 128), cv.f32(HC, 128)]
            wstv = [cv.f32(HC, 256), cv.f32(HC, 256)]
            otmp = [cv.bf16(2048), cv.bf16(2048)]
            vtmp = [cv.bf16(512) for _ in range(3)]

            v_wdma, v_cast, v_mm, v_ev, v_stj = {}, {}, {}, {}, {}
            def wdma(j):
                w = []
                if j >= 2:
                    w.append(("sG", v_cast[j - 2]))
                v_wdma[j] = ctx.dma("sync", wst[j % 2],
                                    qkv_d.ap()[:, j * 128:(j + 1) * 128].rearrange(
                                        "(hc p) c -> p hc c", p=128), w)
            def wcast(j):
                w = [("sS", v_wdma[j])]
                if j >= 2:
                    w.append(("sP", v_mm[(j - 2, 3)]))
                v_cast[j] = ctx.do("gpsimd", lambda e, j=j: e.tensor_copy(
                    wjbf[j % 2], wst[j % 2]), w, "sG")
            wdma(0); wdma(1)
            for j in range(32):
                wcast(j)
                if j + 2 < 32:
                    wdma(j + 2)
                for tc in range(4):
                    gi = j * 4 + tc
                    bank = gi % 2
                    w = [("sG", v_cast[j])]
                    if gi >= 2:
                        w.append(("sC", v_ev[gi - 2]))
                    def f_mm(e, j=j, tc=tc, bank=bank):
                        for hc in range(HC):
                            ins = e.matmul(ps[:, bank, :], wjbf[j % 2][:, hc, :],
                                           h_fm[:, hc, tc * 512:(tc + 1) * 512],
                                           start=(hc == 0), stop=(hc == HC - 1))
                        return ins
                    v_mm[(j, tc)] = ctx.do("tensor", f_mm, w, "sP")
                    w = [("sP", v_mm[(j, tc)])]
                    if j >= 2:
                        w.append(("sA", v_stj[j - 2]))
                    v_ev[gi] = ctx.do("scalar", lambda e, j=j, tc=tc, bank=bank: e.activation(
                        otmp[j % 2][:, tc * 512:(tc + 1) * 512], ps[:, bank, :], AF.Copy),
                        w, "sC")
                if j < 16:
                    for tc in range(4):
                        v_stj[j] = ctx.dma("scalar", q_dram.ap()[tc, j, :, :],
                                           otmp[j % 2][:, tc * 512:(tc + 1) * 512],
                                           [("sC", v_ev[j * 4 + tc])])
                else:
                    jj = j - 16
                    v_stj[j] = ctx.dma("scalar", kv_own.ap()[jj * 128:(jj + 1) * 128, :],
                                       otmp[j % 2], [("sC", v_ev[j * 4 + 3])])
            v_wdv, v_cstv, v_mmv, v_evv, v_stv = {}, {}, {}, {}, {}
            def wdmav(vc4, half):
                i = vc4 * 2 + half
                w = []
                if i >= 2:
                    w.append(("sG", v_cstv[i - 2]))
                col0 = 2 * D + vc4 * 512 + half * 256
                v_wdv[i] = ctx.dma("sync", wstv[i % 2],
                                   qkv_d.ap()[:, col0:col0 + 256].rearrange(
                                       "(hc p) c -> p hc c", p=128), w)
            def wcastv(vc4, half):
                i = vc4 * 2 + half
                w = [("sS", v_wdv[i])]
                if vc4 >= 2:
                    w.append(("sP", v_mmv[(vc4 - 2, TB - 1)]))
                v_cstv[i] = ctx.do("gpsimd", lambda e, vc4=vc4, half=half, i=i: e.tensor_copy(
                    wvbf[vc4 % 2][:, :, half * 256:half * 256 + 256], wstv[i % 2]), w, "sG")
            wdmav(0, 0); wdmav(0, 1)
            for vc4 in range(4):
                for h in range(2):
                    wcastv(vc4, h)
                    if vc4 + 1 < 4:
                        wdmav(vc4 + 1, h)
                for tb in range(TB):
                    gi = vc4 * TB + tb
                    bank = gi % 2
                    w = [("sG", v_cstv[vc4 * 2 + 1])]
                    if gi >= 2:
                        w.append(("sC", v_evv[gi - 2]))
                    def f_mm(e, vc4=vc4, tb=tb, bank=bank):
                        for hc in range(HC):
                            ins = e.matmul(ps[:, bank, :],
                                           h_fm[:, hc, tb * 128:(tb + 1) * 128],
                                           wvbf[vc4 % 2][:, hc, :],
                                           start=(hc == 0), stop=(hc == HC - 1))
                        return ins
                    v_mmv[(vc4, tb)] = ctx.do("tensor", f_mm, w, "sP")
                    w = [("sP", v_mmv[(vc4, tb)])]
                    if gi >= 3:
                        w.append(("sA", v_stv[gi - 3]))
                    v_evv[gi] = ctx.do("scalar", lambda e, gi=gi, bank=bank: e.activation(
                        vtmp[gi % 3], ps[:, bank, :], AF.Copy), w, "sC")
                    v_stv[gi] = ctx.dma("scalar",
                                        kv_own.ap()[D + tb * 128:D + (tb + 1) * 128,
                                                    vc4 * 512:(vc4 + 1) * 512],
                                        vtmp[gi % 3], [("sC", v_evv[gi])])
            if dump == "nocc":
                v_cc = ctx.dma("gpsimd_noop_placeholder", None, None) if False else None
                cvq = Carver(arena, ARENA_N, PH_OFF + 12000)
                tmp = cvq.bf16(2048)
                vp = ctx.cnt["sA"]
                for rr in range(0, 2 * D, 128):
                    vk = ctx.dma("scalar", tmp, kv_own.ap()[rr:rr + 128, :],
                                 [("sA", vp)])
                    vp = ctx.dma("scalar", kv_all.ap()[rr:rr + 128, :], tmp,
                                 [("sA", vk)])
                v_cc = ctx.do("gpsimd", lambda e: e.engine_nop(),
                              [("sA", vp)], "sX")
                return v_cc
            ctx.barrier()
            v_cc = None
            for ch in range(4):
                w = [("sA", v_stv[4 * TB - 1])]
                if v_cc is not None:
                    w.append(("sX", v_cc))
                v_cc = ctx.do("gpsimd", lambda e, ch=ch: e.collective_compute(
                    "AllGather", ALU.bypass, [[0, 1], [2, 3], [4, 5], [6, 7]],
                    [kv_own.ap()[ch * 1024:(ch + 1) * 1024, :].opt()],
                    [kv_all.ap()[ch * 2048:(ch + 1) * 2048, :].opt()]),
                    w, "sX")
            return v_cc

        def attn_phase(v_cc):
            cv = Carver(arena, ARENA_N)
            qtile = cv.bf16(HC, 512)
            scr = cv.bf16(4, S_FULL)
            pT = cv.bf16(32, 512)
            mrow = [cv.bf16(S_FULL), cv.bf16(S_FULL)]
            ktile = [cv.bf16(HC, 256), cv.bf16(HC, 256)]
            vtile = [cv.bf16(8, 512), cv.bf16(8, 512)]
            aot = [cv.bf16(512), cv.bf16(512)]
            aofm = cv.bf16(HC, 512)
            v_ao = {}
            for p_ in range(4):
                ctx.barrier()
                v_q = ctx.dma("sync", qtile,
                              q_dram.ap()[p_].rearrange("j p t -> p j t"),
                              [("sX", v_cc)])
                v_mrow, v_msk, v_exp = {}, {}, {}
                for i in range(2):
                    v_mrow[i] = ctx.dma("scalar", mrow[i % 2], mask_d.ap()[p_ * 4 + i], [])
                v_k, v_sc, v_evsc = {}, {}, {}
                for kt in range(16):
                    w = [("sX", v_cc)]
                    if kt >= 2:
                        w.append(("sP", v_sc[(kt - 2, 3)]))
                    half = kt // 8
                    co = (kt % 8) * 256
                    r0 = half * 1024
                    r1 = 2048 + half * 1024
                    ctx.dma("sync", ktile[kt % 2][:, 0:8, :],
                            kv_all.ap()[r0:r0 + 1024, co:co + 256].rearrange(
                                "(j p) t -> p j t", p=128), w)
                    v_k[kt] = ctx.dma("sync", ktile[kt % 2][:, 8:16, :],
                                      kv_all.ap()[r1:r1 + 1024, co:co + 256].rearrange(
                                          "(j p) t -> p j t", p=128), [])
                    for i in range(4):
                        gi = kt * 4 + i
                        bank = gi % 2
                        w = [("sS", v_k[kt]), ("sS", v_q)]
                        if gi >= 2:
                            w.append(("sV", v_evsc[gi - 2]))
                        def f_mm(e, i=i, kt=kt, bank=bank):
                            for hc in range(HC):
                                ins = e.matmul(ps[:, bank, 0:256],
                                               qtile[:, hc, i * 128:(i + 1) * 128],
                                               ktile[kt % 2][:, hc, :],
                                               start=(hc == 0), stop=(hc == HC - 1))
                            return ins
                        v_sc[(kt, i)] = ctx.do("tensor", f_mm, w, "sP")
                        v_evsc[gi] = ctx.do("vector", lambda e, i=i, kt=kt, bank=bank:
                                            e.tensor_scalar(scr[:, i, kt * 256:(kt + 1) * 256],
                                                            ps[:, bank, 0:256], SCALE, None,
                                                            ALU.mult),
                                            [("sP", v_sc[(kt, i)])], "sV")
                v_pte_last = None
                for i in range(4):
                    b = p_ * 4 + i
                    w = [("sV", v_evsc[15 * 4 + i]), ("sA", v_mrow[i])]
                    if i > 0:
                        w.append(("sC", v_exp[i - 1]))
                    def f_mask(e, i=i):
                        e.tensor_tensor(scr[:, i, :], scr[:, i, :], mrow[i % 2], ALU.add)
                        return e.reduce_max(stats[:, ST_NEGMAX:ST_NEGMAX + 1],
                                            scr[:, i, :], AX.X, negate=True)
                    v_msk[i] = ctx.do("vector", f_mask, w, "sV")
                    if i + 2 < 4:
                        v_mrow[i + 2] = ctx.dma("scalar", mrow[i % 2],
                                                mask_d.ap()[p_ * 4 + i + 2],
                                                [("sV", v_msk[i])])
                    v_exp[i] = ctx.do("scalar", lambda e, i=i: e.activation(
                        scr[:, i, :], scr[:, i, :], AF.Exp,
                        bias=stats[:, ST_NEGMAX:ST_NEGMAX + 1]),
                        [("sV", v_msk[i])], "sC")
                    ctx.do("vector", lambda e, i=i: e.reduce_sum(
                        stats[:, 12 + i:13 + i], scr[:, i, :], AX.X),
                        [("sC", v_exp[i])], "sV")
                    for cc in range(32):
                        def dst(e, psrc, i=i, cc=cc):
                            return e.activation(pT[:, cc, i * 128:(i + 1) * 128], psrc, AF.Copy)
                        _, v_pte_last = ptranspose(scr[:, i, cc * 128:(cc + 1) * 128], dst,
                                                   w_tr=[("sC", v_exp[i])])
                    ctx.do("vector", lambda e, i=i: e.reciprocal(
                        stats[:, ST_RDEN + i:ST_RDEN + i + 1], stats[:, 12 + i:13 + i]),
                        [("sC", v_pte_last)], "sV")
                v_v, v_av, v_ave, v_aotr = {}, {}, {}, {}
                for vq in range(4):
                    for tg in range(4):
                        gi = vq * 4 + tg
                        w = []
                        if gi >= 2:
                            w.append(("sP", v_av[gi - 2]))
                        half = tg // 2
                        r0 = (2 + (tg % 2)) * 2048 + half * 1024
                        v_v[gi] = ctx.dma("sync", vtile[gi % 2],
                                          kv_all.ap()[r0:r0 + 1024,
                                                      vq * 512:(vq + 1) * 512].rearrange(
                                              "(c p) v -> p c v", p=128), w)
                        w = [("sS", v_v[gi]), ("sC", v_pte_last)]
                        if tg == 0 and vq > 0:
                            w.append(("sC", v_ave[(vq - 1, 3)]))
                        def f_av(e, tg=tg, gi=gi):
                            for c8 in range(8):
                                tkc = tg * 8 + c8
                                for i in range(4):
                                    ins = e.matmul(ps[:, i, :],
                                                   pT[:, tkc, i * 128:(i + 1) * 128],
                                                   vtile[gi % 2][:, c8, :],
                                                   start=(tkc == 0), stop=(tkc == 31),
                                                   skip_group_check=True)
                            return ins
                        v_av[gi] = ctx.do("tensor", f_av, w, "sP")
                    for i in range(4):
                        ai = vq * 4 + i
                        w = [("sP", v_av[vq * 4 + 3])]
                        if ai >= 2:
                            w.append(("sP", v_aotr[ai - 2]))
                        v_ave[(vq, i)] = ctx.do("scalar", lambda e, i=i, ai=ai: e.activation(
                            aot[ai % 2], ps[:, i, :], AF.Identity,
                            scale=stats[:, ST_RDEN + i:ST_RDEN + i + 1]), w, "sC")
                        for cc in range(4):
                            def dst(e, psrc, vq=vq, cc=cc, i=i):
                                return e.activation(aofm[:, vq * 4 + cc,
                                                         i * 128:(i + 1) * 128], psrc, AF.Copy)
                            v_aotr[ai], _ = ptranspose(
                                aot[ai % 2][:, cc * 128:(cc + 1) * 128], dst,
                                w_tr=[("sC", v_ave[(vq, i)])])
                v_ao[p_] = ctx.dma("scalar",
                                   ao_dram.ap()[:, p_ * 512:(p_ + 1) * 512].rearrange(
                                       "(vc p) t -> p vc t", p=128), aofm,
                                   [("sC", tr["done"][tr["i"] - 1])])
            return v_ao

        def oproj_phase(v_ao):
            cv = Carver(arena, ARENA_N, PH_OFF)
            wobf = cv.bf16(HC, 2048)
            wst = [cv.f32(HC, 256), cv.f32(HC, 256)]
            aotl = [cv.bf16(HC, 128), cv.bf16(HC, 128)]
            r1b = [cv.f32(512), cv.f32(512)]
            r2b = [cv.f32(512), cv.f32(512)]

            v_wd, v_wc = {}, {}
            for k in range(8):
                w = [] if k < 2 else [("sG", v_wc[k - 2])]
                v_wd[k] = ctx.dma("sync", wst[k % 2],
                                  wo_d.ap()[:, k * 256:(k + 1) * 256].rearrange(
                                      "(hc p) c -> p hc c", p=128), w)
                v_wc[k] = ctx.do("gpsimd", lambda e, k=k: e.tensor_copy(
                    wobf[:, :, k * 256:(k + 1) * 256], wst[k % 2]),
                    [("sS", v_wd[k])], "sG")
            v_at, v_mm, v_r1, v_r2 = {}, {}, {}, {}
            for tb in range(TB):
                w = [("sA", v_ao[3])]
                if tb >= 2:
                    w.append(("sP", v_mm[(tb - 2, 3)]))
                v_at[tb] = ctx.dma("sync", aotl[tb % 2],
                                   ao_dram.ap()[:, tb * 128:(tb + 1) * 128].rearrange(
                                       "(vc p) t -> p vc t", p=128), w)
                for hh in range(4):
                    gi = tb * 4 + hh
                    bank = gi % 2
                    w = []
                    if gi >= 2:
                        w.append(("sV", v_r2[gi - 2]))
                    v_r1[gi] = ctx.dma("scalar", r1b[gi % 2],
                                       r_dram.ap()[tb * 128:(tb + 1) * 128,
                                                   hh * 512:(hh + 1) * 512], w)
                    w = [("sS", v_at[tb]), ("sG", v_wc[7])]
                    if gi >= 2:
                        w.append(("sV", v_r2[gi - 2]))
                    def f_mm(e, tb=tb, hh=hh, bank=bank):
                        for vc in range(HC):
                            ins = e.matmul(ps[:, bank, :], aotl[tb % 2][:, vc, :],
                                           wobf[:, vc, hh * 512:(hh + 1) * 512],
                                           start=(vc == 0), stop=(vc == HC - 1))
                        return ins
                    v_mm[(tb, hh)] = ctx.do("tensor", f_mm, w, "sP")
                    v_r2[gi] = ctx.do("vector", lambda e, gi=gi, bank=bank: e.tensor_tensor(
                        r2b[gi % 2], ps[:, bank, :], r1b[gi % 2], ALU.add),
                        [("sP", v_mm[(tb, hh)]), ("sA", v_r1[gi])], "sV")
                    ctx.dma("scalar",
                            r_dram.ap()[tb * 128:(tb + 1) * 128, hh * 512:(hh + 1) * 512],
                            r2b[gi % 2], [("sV", v_r2[gi])])

        if "ln" in phases:
            h_fm = ln_phase(x_d, with_gb=False)
            ctx.barrier()
            cvd = Carver(arena, ARENA_N, PH_OFF)
            t = cvd.f32(2048)
            vprev = 0
            for hcd in range(HC):
                vv = ctx.do("vector", lambda e, hcd=hcd: e.tensor_copy(t, h_fm[:, hcd, :]),
                            [("sA", vprev)] if vprev else [], "sV")
                vprev = ctx.dma("scalar", out_d.ap()[hcd * 128:(hcd + 1) * 128, :], t,
                                [("sV", vv)])
        if "ffn1" in phases:
            ffn(x_d, r_dram if (len(phases) > 1 or dump == "r")
                else out_d)
        if "lnxe" in phases:
            ffn(x_d, r_dram)
            h_fm = ln_phase(x_d, with_gb=True)
            ctx.barrier()
            cvd = Carver(arena, ARENA_N, PH_OFF)
            t = cvd.f32(2048)
            vprev = 0
            for hcd in range(HC):
                vv = ctx.do("vector", lambda e, hcd=hcd: e.tensor_copy(t, h_fm[:, hcd, :]),
                            [("sA", vprev)] if vprev else [], "sV")
                vprev = ctx.dma("scalar", out_d.ap()[hcd * 128:(hcd + 1) * 128, :], t,
                                [("sV", vv)])
        if "lnre" in phases:
            h_fm0 = ln_phase(x_d, with_gb=False)
            ctx.barrier()
            cc0 = Carver(arena, ARENA_N, CBLK_OFF)
            cblk0 = (cc0.f32(2048, parts=8), cc0.bf16(2048, parts=8), cc0.f32(TB, 8))
            router_phase(h_fm0, cblk0)
            ctx.barrier()
            h_fm = ln_phase(x_d, with_gb=True)
            ctx.barrier()
            cvd = Carver(arena, ARENA_N, PH_OFF)
            t = cvd.f32(2048)
            vprev = 0
            for hcd in range(HC):
                vv = ctx.do("vector", lambda e, hcd=hcd: e.tensor_copy(t, h_fm[:, hcd, :]),
                            [("sA", vprev)] if vprev else [], "sV")
                vprev = ctx.dma("scalar", out_d.ap()[hcd * 128:(hcd + 1) * 128, :], t,
                                [("sV", vv)])
        if "lnx2" in phases:
            ln_phase(x_d, with_gb=False)
            ctx.barrier()
            h_fm = ln_phase(x_d, with_gb=True)
            ctx.barrier()
            cvd = Carver(arena, ARENA_N, PH_OFF)
            t = cvd.f32(2048)
            vprev = 0
            for hcd in range(HC):
                vv = ctx.do("vector", lambda e, hcd=hcd: e.tensor_copy(t, h_fm[:, hcd, :]),
                            [("sA", vprev)] if vprev else [], "sV")
                vprev = ctx.dma("scalar", out_d.ap()[hcd * 128:(hcd + 1) * 128, :], t,
                                [("sV", vv)])
        if "lngb" in phases:
            h_fm = ln_phase(x_d, with_gb=True)
            ctx.barrier()
            cvd = Carver(arena, ARENA_N, PH_OFF)
            t = cvd.f32(2048)
            vprev = 0
            for hcd in range(HC):
                vv = ctx.do("vector", lambda e, hcd=hcd: e.tensor_copy(t, h_fm[:, hcd, :]),
                            [("sA", vprev)] if vprev else [], "sV")
                vprev = ctx.dma("scalar", out_d.ap()[hcd * 128:(hcd + 1) * 128, :], t,
                                [("sV", vv)])
        if "lnattn" in phases:
            h_fm = ln_phase(r_dram, with_gb=True)
            ctx.barrier()
            cvd = Carver(arena, ARENA_N, PH_OFF)
            t = cvd.f32(2048)
            vprev = 0
            for hcd in range(HC):
                vv = ctx.do("vector", lambda e, hcd=hcd: e.tensor_copy(t, h_fm[:, hcd, :]),
                            [("sA", vprev)] if vprev else [], "sV")
                vprev = ctx.dma("scalar", out_d.ap()[hcd * 128:(hcd + 1) * 128, :], t,
                                [("sV", vv)])
            cvln = Carver(arena, ARENA_N, PH_OFF)
            rb0 = cvln.f32(2048); rb1 = cvln.f32(2048)
            xb0 = cvln.bf16(2048); xb1 = cvln.bf16(2048)
            vk = ctx.dma("scalar", out_d.ap()[0:128, 0:48], stats[:, :], [("sA", vprev)])
            vv = ctx.do("vector", lambda e: e.tensor_copy(t, xb0), [("sA", vk)], "sV")
            vk = ctx.dma("scalar", out_d.ap()[128:256, :], t, [("sV", vv)])
            vv = ctx.do("vector", lambda e: e.tensor_copy(t, xb1), [("sA", vk)], "sV")
            vk = ctx.dma("scalar", out_d.ap()[256:384, :], t, [("sV", vv)])
            vv = ctx.do("vector", lambda e: e.tensor_copy(t, rb0), [("sA", vk)], "sV")
            vk = ctx.dma("scalar", out_d.ap()[384:512, :], t, [("sV", vv)])
        if "attn" in phases or "attnq" in phases:
            h_fm = ln_phase(r_dram, with_gb=True)
            ctx.barrier()
            v_cc = qkv_phase(h_fm)
            ctx.barrier()
            if dump == "dbgq":
                cvd = Carver(arena, ARENA_N, PH_OFF)
                t = cvd.f32(2048)
                t2 = cvd.bf16(2048)
                vprev = 0
                for row, src in ((0, kv_all.ap()[0:128, :]),
                                 (128, kv_all.ap()[4096:4224, :]),
                                 (384, kv_all.ap()[1024:1152, :]),
                                 (512, kv_all.ap()[5120:5248, :])):
                    vk = ctx.dma("scalar", t2, src, [("sA", vprev)] if vprev else [])
                    vv = ctx.do("vector", lambda e: e.tensor_copy(t, t2), [("sA", vk)], "sV")
                    vprev = ctx.dma("scalar", out_d.ap()[row:row + 128, :], t, [("sV", vv)])
                vq2 = ctx.dma("scalar", t2.rearrange("p (a b) -> p a b", a=4),
                              q_dram.ap()[0:4, 0, :, :].rearrange("a p t -> p a t"),
                              [("sA", vprev)])
                vv = ctx.do("vector", lambda e: e.tensor_copy(t, t2), [("sA", vq2)], "sV")
                ctx.dma("scalar", out_d.ap()[256:384, :], t, [("sV", vv)])
        if "attn" in phases:
            v_ao = attn_phase(v_cc)
            ctx.barrier()
            if dump == "dbga":
                cvd = Carver(arena, ARENA_N, PH_OFF)
                t = cvd.f32(2048)
                t2 = cvd.bf16(2048)
                vprev = 0
                for rr in range(4):
                    vk = ctx.dma("scalar", t2, ao_dram.ap()[rr * 128:(rr + 1) * 128, :],
                                 [("sA", vprev)] if vprev else [])
                    vv = ctx.do("vector", lambda e: e.tensor_copy(t, t2), [("sA", vk)], "sV")
                    vprev = ctx.dma("scalar", out_d.ap()[rr * 128:(rr + 1) * 128, :], t,
                                    [("sV", vv)])
                ctx.dma("scalar", out_d.ap()[512:640, 0:16], stats[:, 0:16],
                        [("sA", vprev)])
                cva = Carver(arena, ARENA_N)
                qtile2 = cva.bf16(HC, 512)
                scr2 = cva.bf16(4, S_FULL)
                for i4 in range(4):
                    for hh in range(2):
                        vv = ctx.do("vector", lambda e, i4=i4, hh=hh: e.tensor_copy(
                            t, scr2[:, i4, hh * 2048:(hh + 1) * 2048]),
                            [("sA", vprev)], "sV")
                        vprev = ctx.dma("scalar",
                                        out_d.ap()[640 + (i4 * 2 + hh) * 128:
                                                   640 + (i4 * 2 + hh + 1) * 128, :],
                                        t, [("sV", vv)])
                ctx.barrier()
            oproj_phase(v_ao)
            ctx.barrier()
        if "ffn2" in phases:
            ffn(r_dram, out_d)
        if dump == "r":
            ctx.barrier()
            cvd = Carver(arena, ARENA_N)
            t = cvd.f32(TB, 2048)
            v = ctx.dma("scalar", t, r_dram.ap().rearrange("(tb p) o -> p tb o", p=128))
            ctx.dma("scalar", out_d.ap().rearrange("(tb p) o -> p tb o", p=128), t,
                    [("sA", v)])
        if dump == "dbg1":
            ctx.barrier()
            cvd = Carver(arena, ARENA_N, PH_OFF)
            cc = Carver(arena, ARENA_N, CBLK_OFF)
            c_sb = cc.f32(2048, parts=8)
            c_bf = cc.bf16(2048, parts=8)
            c05 = cc.f32(TB, 8)
            h_fm = Carver(arena, ARENA_N).bf16(HC, 2048)
            t = cvd.f32(2048)
            v1 = ctx.dma("scalar", out_d.ap()[0:8, :], c_sb)
            v2 = ctx.dma("scalar", out_d.ap()[8:136, 0:128],
                         c05.rearrange("p a b -> p (a b)"))
            vprev = 0
            for hcd in range(4):
                vv = ctx.do("vector", lambda e, hcd=hcd: e.tensor_copy(t, h_fm[:, hcd, :]),
                            [("sA", vprev)] if vprev else [], "sV")
                vprev = ctx.dma("scalar", out_d.ap()[256 + hcd * 128:384 + hcd * 128, :], t,
                                [("sV", vv)])
        ctx.barrier()

    with nc.Block() as block:
        for en in ENGINES:
            def body(eng, en=en):
                emit_all(Ctx(nc, en, eng, sems))
            getattr(block, en)(body)
    return nc


_CACHE = {}


def _host_layouts(qkv, wo, gamma, beta, router_w, router_b, expert_w, expert_b):
    f32 = np.float32
    rw3 = np.asarray(router_w, f32).reshape(16, 128, 8, 2048)
    rwt = np.ascontiguousarray(rw3.transpose(3, 0, 2, 1).reshape(2048, 16384))
    rbv = np.asarray(router_b, f32).reshape(16, 128, 8)
    rbias = np.ascontiguousarray(rbv.transpose(0, 2, 1).reshape(128, 128).T)
    ewt = np.ascontiguousarray(np.asarray(expert_w, f32).transpose(0, 2, 1))
    return dict(
        rwt=rwt, rbias=rbias, ewt=ewt,
        eb=np.ascontiguousarray(np.asarray(expert_b, f32)),
        qkvw=np.ascontiguousarray(np.asarray(qkv, f32)),
        wo=np.ascontiguousarray(np.asarray(wo, f32)),
        gam=np.ascontiguousarray(np.asarray(gamma, f32).reshape(16, 128).T),
        bet=np.ascontiguousarray(np.asarray(beta, f32).reshape(16, 128).T),
    )


def _maskrows(half):
    m = np.zeros((TB, 128, S_FULL), np.float32)
    for b in range(TB):
        gb = half * TB + b
        lim = gb * 128 + np.arange(128)[:, None]
        m[b] = np.where(np.arange(S_FULL)[None, :] <= lim, 0.0, -1000.0)
    return np.ascontiguousarray(m.astype(ml_dtypes.bfloat16))


def kernel(L0_attn_x, L0_attn_qkv, L0_attn_o_proj, L0_attn_gamma, L0_attn_beta,
           L0_ffn_router_w, L0_ffn_router_b, L0_ffn_expert_w, L0_ffn_expert_b,
           _phases=("ffn1", "attn", "ffn2"), _dump=None):
    key = (_phases, _dump)
    if key not in _CACHE:
        _CACHE[key] = build(_phases, _dump)
    nc = _CACHE[key]
    shared = _host_layouts(L0_attn_qkv, L0_attn_o_proj, L0_attn_gamma, L0_attn_beta,
                           L0_ffn_router_w, L0_ffn_router_b, L0_ffn_expert_w,
                           L0_ffn_expert_b)
    x = np.asarray(L0_attn_x, np.float32)
    masks = [_maskrows(0), _maskrows(1)]
    in_maps = []
    for c in range(8):
        b, half = c // 2, c % 2
        m = dict(shared)
        m["x"] = np.ascontiguousarray(x[b, half * P:(half + 1) * P, :])
        m["maskrows"] = masks[half]
        in_maps.append(m)
    res = run_bass_kernel_spmd(nc, in_maps, core_ids=list(range(8)))
    out = np.empty((4, S_FULL, D), np.float32)
    for c in range(8):
        b, half = c // 2, c % 2
        out[b, half * P:(half + 1) * P, :] = res.results[c]["out"]
    return out
